# version 01
# baseline (speedup 1.0000x reference)
import contextlib
import numpy as np
import ml_dtypes
import concourse.bass as bass
import concourse.mybir as mybir
from concourse.bass import ds
from concourse.expressions_rust import RuntimeValue
from concourse.bass_utils import run_bass_kernel_spmd

F32 = mybir.dt.float32
BF16 = mybir.dt.bfloat16
U32 = mybir.dt.uint32
AF = mybir.ActivationFunctionType
ALU = mybir.AluOpType
AX = mybir.AxisListType.X

B, S, D = 2, 16384, 2048
T = B * S
NCORE = 8
TPC = T // NCORE
HQK, HV, HD = 16, 32, 128
NH, NV = 2, 4
CH = NV * HD
CONV_K = 4
CHUNK = 128
MEGA_S = 256
NMC = S // MEGA_S
NCS = MEGA_S // CHUNK
FFN_TOK = 512
NFM = TPC // FFN_TOK
NEXP = 16
EPS_LN = 1e-5
EPS_RMS = 1e-6
EPS_TOP = 1e-9
QKVZ_PC = 1536
PROJ_PC = 1544
NLEV = 7

ENGINES = ("t", "v", "sc", "g", "sy")


class Stream:
    def __init__(self, name):
        self.name = name
        self.sem = None
        self.n = 0


class Ctx:

    def __init__(self, nc, es):
        self.nc = nc
        self.es = es
        self.plan = []
        self.streams = {}
        self.estream = {e: self.stream("E" + e) for e in ENGINES}
        self.pending = {e: [] for e in ENGINES}
        self.holder = {}

    def stream(self, name):
        if name not in self.streams:
            self.streams[name] = Stream(name)
        return self.streams[name]

    def op(self, eng, fn, waits=(), inc=None, dma=False, tag=None):
        ws = list(self.pending[eng])
        self.pending[eng] = []
        for w in waits:
            if w is None:
                continue
            if isinstance(w, Stream):
                ws.append((w, w.n))
            else:
                ws.append((w[0], w[1]))
        rec = {"eng": eng, "fn": fn, "waits": ws, "inc": None, "tag": tag}
        ret = None
        if inc is True:
            st = self.estream[eng]
            st.n += 1
            rec["inc"] = (st, 1)
            rec["inc_val"] = st.n
            ret = (st, st.n)
        elif isinstance(inc, Stream):
            k = 16 if dma else 1
            inc.n += k
            rec["inc"] = (inc, k)
            ret = (inc, inc.n)
        self.plan.append(rec)
        return ret

    def dop(self, eng, fn, waits=(), stream=None, tag=None):
        if stream is None:
            stream = self.stream("D" + eng)
        ws = list(waits)
        if stream.n > 0:
            ws.append((stream, stream.n))
        return self.op(eng, fn, waits=ws, inc=stream, dma=True, tag=tag)

    def mark(self, eng):
        st = self.estream[eng]
        return (st, st.n)

    def barrier(self, engines=ENGINES, extra=()):
        ms = [(st, st.n) for st in self.estream.values()]
        for w in extra:
            ms.append((w, w.n) if isinstance(w, Stream) else w)
        for e in engines:
            self.pending[e] = self.pending[e] + ms

    def emit(self, block):
        for st in self.streams.values():
            st.sem = self.es.enter_context(self.nc.semaphore(st.name))
        plan = self.plan

        def run_engine(eng_name, eng):
            seen = {}
            own = self.estream.get(eng_name)
            self_serial = eng_name in ("v", "sc")
            last_inc = 0
            for rec in plan:
                if rec["eng"] != eng_name:
                    continue
                if self_serial and last_inc > 0 and seen.get(own.name, 0) < last_inc:
                    eng.wait_ge(own.sem, last_inc)
                    seen[own.name] = last_inc
                for stream, val in rec["waits"]:
                    if val > 0 and seen.get(stream.name, 0) < val:
                        eng.wait_ge(stream.sem, val)
                        seen[stream.name] = val
                inst = rec["fn"](eng)
                if inst is not None and rec["tag"]:
                    try:
                        inst.annotate(rec["tag"])
                    except Exception:
                        pass
                if rec["inc"] is not None:
                    stream, k = rec["inc"]
                    assert inst is not None, f"inc but fn returned None: {rec['tag']}"
                    inst.then_inc(stream.sem, k)
                    if stream is own:
                        last_inc = rec.get("inc_val", last_inc)

        @block.tensor
        def _(e):
            run_engine("t", e)

        @block.vector
        def _(e):
            run_engine("v", e)

        @block.scalar
        def _(e):
            run_engine("sc", e)

        @block.gpsimd
        def _(e):
            run_engine("g", e)

        @block.sync
        def _(e):
            run_engine("sy", e)


def r8(ap1024):
    return ap1024.rearrange("p (r m) -> p r m", r=8)


def build_program(taps=(), upto="all"):
    nc = bass.Bass("TRN2", target_bir_lowering=False, debug=False)
    es = contextlib.ExitStack()

    def dram_in(name, shape, dtype=F32):
        return nc.dram_tensor(name, list(shape), dtype, kind="ExternalInput").ap()

    ih = {}
    ih["x_my"] = dram_in("x_my", (TPC, D))
    ih["wproj"] = dram_in("wproj", (D, PROJ_PC))
    ih["convw"] = dram_in("convw", (1024, CONV_K))
    ih["dtb"] = dram_in("dtb", (1, NV))
    ih["alog"] = dram_in("alog", (1, NV))
    ih["normw"] = dram_in("normw", (1, HD))
    ih["wout"] = dram_in("wout", (4, 4, 128, 512))
    ih["wrT"] = dram_in("wrT", (D, NEXP))
    ih["rbias"] = dram_in("rbias", (1, NEXP))
    ih["wg"] = dram_in("wg", (NEXP, 8, 128, 2048))
    ih["wu"] = dram_in("wu", (NEXP, 8, 128, 2048))
    ih["wd"] = dram_in("wd", (NEXP, 4, 8, 128, 512))
    ih["swg"] = dram_in("swg", (8, 128, 2048))
    ih["swu"] = dram_in("swu", (8, 128, 2048))
    ih["swd"] = dram_in("swd", (4, 8, 128, 512))
    out_h = nc.dram_tensor("out", [TPC, D], F32, kind="ExternalOutput").ap()

    dr = {}
    dr["ag_in"] = nc.dram_tensor("ag_in", [TPC, D], BF16, kind="Internal").ap()
    dr["ln1_all"] = nc.dram_tensor("ln1_all", [T, D], BF16, kind="Internal").ap()
    dr["wosl_bf"] = nc.dram_tensor("wosl_bf", [4, 4, 128, 512], BF16, kind="Internal").ap()
    for _j in range(4):
        dr["rs_in%d" % _j] = nc.dram_tensor("rs_in%d" % _j, [T, 512], F32, kind="Internal").ap()
        dr["rs_out%d" % _j] = nc.dram_tensor("rs_out%d" % _j, [TPC, 512], F32, kind="Internal").ap()
    dr["ngscr"] = nc.dram_tensor("ngscr", [32, 128], F32, kind="Internal").ap()
    dr["wscr"] = nc.dram_tensor("wscr", [16, 512], BF16, kind="Internal").ap()
    dr["bar_in"] = nc.inline_tensor(np.zeros((8, 4), np.float32), "bar_in").ap()
    dr["bar_out"] = nc.dram_tensor("bar_out", [64, 4], F32, kind="Internal").ap()
    dr["bar_out2"] = nc.dram_tensor("bar_out2", [64, 4], F32, kind="Internal").ap()
    dr["wg_bf"] = nc.dram_tensor("wg_bf", [NEXP, 8, 128, 2048], BF16, kind="Internal").ap()
    dr["wu_bf"] = nc.dram_tensor("wu_bf", [NEXP, 8, 128, 2048], BF16, kind="Internal").ap()
    dr["wd_bf"] = nc.dram_tensor("wd_bf", [NEXP, 4, 8, 128, 512], BF16, kind="Internal").ap()
    dr["swg_bf"] = nc.dram_tensor("swg_bf", [8, 128, 2048], BF16, kind="Internal").ap()
    dr["swu_bf"] = nc.dram_tensor("swu_bf", [8, 128, 2048], BF16, kind="Internal").ap()
    dr["swd_bf"] = nc.dram_tensor("swd_bf", [4, 8, 128, 512], BF16, kind="Internal").ap()

    co = {}
    co["I_bf"] = nc.inline_tensor(np.eye(128, dtype=ml_dtypes.bfloat16), "co_Ibf").ap()
    co["I_f"] = nc.inline_tensor(np.eye(128, dtype=np.float32), "co_If").ap()
    co["ones1"] = nc.inline_tensor(np.ones((1, 128), np.float32), "co_ones1").ap()
    co["onesF"] = nc.inline_tensor(np.ones((128, 128), np.float32), "co_onesF").ap()
    co["strilF"] = nc.inline_tensor(np.tril(np.ones((128, 128), np.float32), -1), "co_strilF").ap()
    co["striuF"] = nc.inline_tensor(np.triu(np.ones((128, 128), np.float32), 1), "co_striuF").ap()
    co["strilB"] = nc.inline_tensor(np.tril(np.ones((128, 128)), -1).astype(ml_dtypes.bfloat16), "co_strilB").ap()
    co["triuIB"] = nc.inline_tensor(np.triu(np.ones((128, 128))).astype(ml_dtypes.bfloat16), "co_triuIB").ap()
    co["triuIF"] = nc.inline_tensor(np.triu(np.ones((128, 128), np.float32)), "co_triuIF").ap()

    ctx = Ctx(nc, es)
    bld = Builder(ctx, ih, out_h, dr, co, set(taps))
    bld.upto = upto
    bld.build()
    block = es.enter_context(nc.Block())
    ctx.emit(block)
    es.close()
    return nc, bld.tap_names


class Builder:
    def __init__(self, ctx, ih, out_h, dr, co, taps):
        self.ctx = ctx
        self.nc = ctx.nc
        self.ih = ih
        self.out_h = out_h
        self.dr = dr
        self.co = co
        self.taps = taps
        self.tap_names = []
        self.c = {}
        self.ms = {}
        self.final_extra = []

    def sb(self, es, name, p, f, dt=F32):
        return es.enter_context(self.nc.sbuf_tensor(name, [p, f], dt))

    def pst(self, es, name, p, f, dt=F32):
        return es.enter_context(self.nc.psum_tensor(name, [p, f], dt))

    def tap(self, name, src_ap, waits):
        if name not in self.taps:
            return
        if not isinstance(src_ap, bass.AP):
            src_ap = src_ap[:, :]
        t = self.nc.dram_tensor("tap_" + name, list(src_ap.shape), src_ap.dtype,
                                kind="ExternalOutput").ap()
        self.tap_names.append("tap_" + name)
        st = self.ctx.stream("tapst")
        m = self.ctx.dop("g", lambda g, s=src_ap, d=t: g.dma_start(out=d, in_=s),
                         waits=waits, stream=st, tag="tap" + name)
        self.final_extra.append(m)

    def build(self):
        ctx = self.ctx
        es = ctx.es
        self.alloc_persistent(es)
        self.emit_p0_pre()
        self.emit_p0b()

        es1 = es.enter_context(contextlib.ExitStack())
        self.a = {}
        self.a["x1"] = self.sb(es1, "a_x1", 128, 2 * D, F32)
        self.a["lno"] = self.sb(es1, "a_lno", 128, 2 * D, BF16)
        self.a["dump"] = self.sb(es1, "a_dump", 128, D, F32)
        self.a["st1"] = self.sb(es1, "a_st1", 128, 16, F32)
        self.emit_p1()
        self.emit_p2()
        es1.close()
        ln1st = ctx.stream("ln1st")
        ctx.barrier(extra=[ln1st])
        self.es3 = es.enter_context(contextlib.ExitStack())
        p1bufs = self.a
        self.alloc_attn(self.es3)
        self.emit_p0_attn()
        self.tap("ln1_all", self.dr["ln1_all"], [self.ms["cc1"]])
        if self.upto == "p2":
            self.finish_early()
            return
        nmc = 1 if self.upto == "p3a" else NMC
        for mc in range(nmc):
            self.emit_p3(mc)
        if self.upto == "p3a":
            hst = ctx.stream("hst")
            for j in range(4):
                self.tap("rs_in%d" % j, self.dr["rs_in%d" % j], [(hst, hst.n)])
            self.tap("cout", self.a["cout"], [self.ctx.mark("sc"), self.ctx.mark("t")])
            self.tap("zst", self.a["zst"], [self.ctx.mark("sc"), self.ctx.mark("v")])
            self.tap("gc", self.a["gc"], [self.ctx.mark("v")])
            self.tap("beta", self.a["beta"], [self.ctx.mark("sc")])
            self.finish_early()
            return
        self.emit_p4()
        for j in range(4):
            self.tap("rs_out%d" % j, self.dr["rs_out%d" % j], [self.ms["cc2"]])
            self.tap("rs_in%d" % j, self.dr["rs_in%d" % j], [self.ms["cc2"]])
        if self.upto == "p4":
            self.finish_early()
            return
        self.es3.close()

        ctx.barrier()
        self.es5 = es.enter_context(contextlib.ExitStack())
        self.alloc_ffn(self.es5)
        self.emit_p5_init()
        for fm in range(NFM):
            self.emit_p5(fm)

        ost = ctx.stream("ost")
        ctx.op("sc", lambda sc: sc.nop(),
               waits=[(ost, ost.n)] + self.final_extra, tag="final_wait")

    def finish_early(self):
        self.ctx.op("sc", lambda sc: sc.nop(), waits=self.final_extra, tag="final_wait")

    def alloc_persistent(self, es):
        c = self.c
        c["I_bf"] = self.sb(es, "c_Ibf", 128, 128, BF16)
        c["I_f"] = self.sb(es, "c_If", 128, 128, F32)
        c["ones1"] = self.sb(es, "c_ones1", 1, 128, F32)
        c["onesF"] = self.sb(es, "c_onesF", 128, 128, F32)
        c["ones1b"] = self.sb(es, "c_ones1b", 1, 128, BF16)
        c["strilF"] = self.sb(es, "c_strilF", 128, 128, F32)
        c["striuF"] = self.sb(es, "c_striuF", 128, 128, F32)
        c["strilB"] = self.sb(es, "c_strilB", 128, 128, BF16)
        c["triuIB"] = self.sb(es, "c_triuIB", 128, 128, BF16)
        c["triuIF"] = self.sb(es, "c_triuIF", 128, 128, F32)
        c["nwrep"] = self.sb(es, "c_nwrep", 128, 128, F32)
        c["dtbrep"] = self.sb(es, "c_dtbrep", 128, NCS * 8, F32)
        c["nAerep"] = self.sb(es, "c_nAerep", 128, NCS * 8, F32)
        c["rbias"] = self.sb(es, "c_rbias", 128, NEXP, F32)
        c["wrT"] = self.sb(es, "c_wrT", 128, 16 * NEXP, BF16)
        c["row4"] = self.sb(es, "c_row4", 1, 256, F32)

    def bc(self, ap128, r):
        return ap128.unsqueeze(1).broadcast_to([128, r, 128])

    def emit_p0_pre(self):
        ctx, c, co, ih = self.ctx, self.c, self.co, self.ih
        ld = ctx.stream("cload")
        def d(dst, src):
            return ctx.dop("sy", lambda sy, a=dst, b=src: sy.dma_start(out=a, in_=b), stream=ld)
        d(c["I_bf"][:, :], co["I_bf"][:, :])
        d(c["onesF"][:, :], co["onesF"][:, :])
        d(c["I_f"][:, :], co["I_f"][:, :])
        d(c["ones1"][:, :], co["ones1"][:, :])
        d(c["strilF"][:, :], co["strilF"][:, :])
        d(c["striuF"][:, :], co["striuF"][:, :])
        d(c["strilB"][:, :], co["strilB"][:, :])
        d(c["triuIB"][:, :], co["triuIB"][:, :])
        d(c["triuIF"][:, :], co["triuIF"][:, :])
        d(c["row4"][0:1, 0:128], ih["normw"][0:1, :])
        ldg0 = ctx.stream("cloadg")
        ctx.dop("g", lambda g: g.dma_start(out=c["ones1b"][0:1, :], in_=co["ones1"][0:1, :]),
                stream=ldg0)
        d(c["row4"][0:1, 128:132], ih["dtb"][0:1, :])
        d(c["row4"][0:1, 132:136], ih["alog"][0:1, :])
        d(c["row4"][0:1, 136:152], ih["rbias"][0:1, :])
        ldg = ctx.stream("cloadg")
        ctx.dop("g", lambda g: g.dma_start(
            out=c["wrT"].ap().rearrange("p (kc e) -> p kc e", kc=16),
            in_=ih["wrT"].rearrange("(kc p) e -> p kc e", p=128)),
            stream=ldg)
        self.ms["cload"] = [(ld, ld.n), (ldg, ldg.n)]

    def emit_p0b(self):
        ctx, ih, dr = self.ctx, self.ih, self.dr
        pc = ctx.stream("precast")
        for e in range(NEXP):
            ctx.dop("g", lambda g, e=e: g.dma_start(out=dr["wg_bf"][e], in_=ih["wg"][e]), stream=pc)
            ctx.dop("g", lambda g, e=e: g.dma_start(out=dr["wu_bf"][e], in_=ih["wu"][e]), stream=pc)
            ctx.dop("g", lambda g, e=e: g.dma_start(out=dr["wd_bf"][e], in_=ih["wd"][e]), stream=pc)
        ctx.dop("g", lambda g: g.dma_start(out=dr["swg_bf"], in_=ih["swg"]), stream=pc)
        ctx.dop("g", lambda g: g.dma_start(out=dr["swu_bf"], in_=ih["swu"]), stream=pc)
        ctx.dop("g", lambda g: g.dma_start(out=dr["swd_bf"], in_=ih["swd"]), stream=pc)
        ctx.dop("g", lambda g: g.dma_start(out=dr["wosl_bf"], in_=ih["wout"]), stream=pc)
        self.ms["precast"] = (pc, pc.n)

    def alloc_attn(self, es):
        a = {}
        sb = self.sb
        a["wproj"] = sb(es, "a_wproj", 128, 16 * PROJ_PC, BF16)
        a["cdiag"] = sb(es, "a_cdiag", 128, 32 * 128, BF16)
        a["cwf"] = sb(es, "a_cwf", 128, 32, F32)
        a["lnfm"] = sb(es, "a_lnfm", 128, 2 * 4096, BF16)
        a["cin"] = sb(es, "a_cin", 128, 16 * (MEGA_S + 3), BF16)
        a["cout"] = sb(es, "a_cout", 128, 16 * MEGA_S, BF16)
        a["zst"] = sb(es, "a_zst", 128, 8 * MEGA_S, BF16)
        a["bafm"] = sb(es, "a_bafm", 8, 2 * 512, F32)
        a["halo"] = sb(es, "a_halo", 128, 48, BF16)
        a["bb"] = sb(es, "a_bb", 128, NCS * 8, F32)
        a["aa"] = sb(es, "a_aa", 128, NCS * 8, F32)
        a["beta"] = sb(es, "a_beta", 128, NCS * 8, F32)
        a["gc"] = sb(es, "a_gc", 128, NCS * 8, F32)
        a["ngc"] = sb(es, "a_ngc", 128, NCS * 8, F32)
        a["egc"] = sb(es, "a_egc", 128, NCS * 8, F32)
        a["gt1"] = sb(es, "a_gt1", 128, NCS * 8, F32)
        a["gt3"] = sb(es, "a_gt3", 128, NCS * 8, F32)
        a["gt2"] = sb(es, "a_gt2", 128, 128, F32)
        a["ngcflat"] = sb(es, "a_ngcflat", 128, NCS * 512, F32)
        a["qkh"] = sb(es, "a_qkh", 128, 1024, BF16)
        a["qkhf"] = sb(es, "a_qkhf", 128, 1024, BF16)
        a["v_tm"] = sb(es, "a_vtm", 128, 1024, BF16)
        a["z_tm"] = sb(es, "a_ztm", 128, 1024, BF16)
        a["ssq8"] = sb(es, "a_ssq8", 128, 32, F32)
        a["D"] = sb(es, "a_D", 128, 1024, F32)
        a["Dm"] = sb(es, "a_Dm", 128, 1024, F32)
        a["edec"] = sb(es, "a_edec", 128, 8, F32)
        a["eglast"] = sb(es, "a_eglast", 128, 8, F32)
        a["be8"] = sb(es, "a_be8", 128, 8, F32)
        a["EL"] = sb(es, "a_EL", 128, 1024, BF16)
        a["EU"] = sb(es, "a_EU", 128, 1024, BF16)
        a["Nm"] = sb(es, "a_Nm", 128, 1024, BF16)
        a["NTm"] = sb(es, "a_NTm", 128, 1024, BF16)
        a["attnT"] = sb(es, "a_attnT", 128, 1024, BF16)
        a["ttmp"] = sb(es, "a_ttmp", 128, 1024, BF16)
        a["kbe"] = sb(es, "a_kbe", 128, 1024, BF16)
        a["kbef"] = sb(es, "a_kbef", 128, 1024, BF16)
        a["qt"] = sb(es, "a_qt", 128, 1024, BF16)
        a["qtf"] = sb(es, "a_qtf", 128, 1024, BF16)
        a["kd"] = sb(es, "a_kd", 128, 1024, BF16)
        a["Fp"] = sb(es, "a_Fp", 128, 2048, BF16)
        a["Pp"] = sb(es, "a_Pp", 128, 2048, BF16)
        a["Q"] = sb(es, "a_Q", 128, NLEV * 1024, BF16)
        a["Y"] = sb(es, "a_Y", 128, 1024, BF16)
        a["Sst"] = sb(es, "a_Sst", 128, 1024, F32)
        a["Sbf"] = sb(es, "a_Sbf", 128, 1024, BF16)
        a["rst8"] = sb(es, "a_rst8", 128, 24, F32)
        a["h_sb"] = sb(es, "a_hsb", 128, 1024, BF16)
        a["wosl"] = sb(es, "a_wosl", 128, 2 * 2048, BF16)
        a["pT"] = self.pst(es, "a_pT", 128, 1024, BF16)
        a["pT2"] = self.pst(es, "a_pT2", 128, 1024, BF16)
        for i in range(1, 7):
            a["pF%d" % i] = self.pst(es, "a_pF%d" % i, 128, 512, F32)
        self.a = a
        self.bank_r = {k: [] for k in
                       ("pT", "pT2", "pF1", "pF2", "pF3", "pF4", "pF5", "pF6")}
        self.pools = a

    def bank_wait(self, name):
        w = self.bank_r[name]
        self.bank_r[name] = []
        return w

    def bank_read(self, name, ms):
        self.bank_r[name].append(ms)

    def TT(self, eng, out, i0, i1, op, waits, tag=None):
        return self.ctx.op(eng, lambda e: e.tensor_tensor(out, i0, i1, op), waits=waits,
                           inc=True, tag=tag)

    def TS(self, eng, out, i0, s1, s2, op0, op1, waits, tag=None):
        if op1 is None:
            return self.ctx.op(eng, lambda e: e.tensor_scalar(out, i0, s1, None, op0),
                               waits=waits, inc=True, tag=tag)
        return self.ctx.op(eng, lambda e: e.tensor_scalar(out, i0, s1, s2, op0, op1),
                           waits=waits, inc=True, tag=tag)

    def STT(self, eng, out, i0, s, i1, op0, op1, waits, tag=None):
        return self.ctx.op(eng, lambda e: e.scalar_tensor_tensor(out, i0, s, i1, op0, op1),
                           waits=waits, inc=True, tag=tag)

    def CP(self, eng, out, src, waits, tag=None):
        if eng == "sc":
            return self.ctx.op("sc", lambda e: e.copy(out, src), waits=waits, inc=True, tag=tag)
        return self.ctx.op(eng, lambda e: e.tensor_copy(out, src), waits=waits, inc=True, tag=tag)

    def TR(self, out_psum, in_sb, ident, waits, tag="tr"):
        return self.ctx.op("t", lambda t: t.transpose(out_psum, in_sb, ident), waits=waits,
                           inc=True, tag=tag)

    def MM(self, pb_ap, lhsT, rhs, start, stop, waits, tag="mm", force_inc=False):
        return self.ctx.op("t", lambda t: t.matmul(pb_ap, lhsT, rhs, start=start, stop=stop),
                           waits=waits, inc=(True if (stop or force_inc) else None), tag=tag)

    def ACT(self, out, in_, func, waits, bias=0.0, scale=1.0, accum_out=None, tag=None):
        return self.ctx.op("sc", lambda sc: sc.activation(out, in_, func, bias=bias,
                                                          scale=scale, accum_out=accum_out),
                           waits=waits, inc=True, tag=tag)

    def emit_p0_attn(self):
        ctx, c, a, ih = self.ctx, self.c, self.a, self.ih
        ld = ctx.stream("cloadg")
        ctx.dop("g", lambda g: g.dma_start(
            out=a["wproj"].ap().rearrange("p (kc m) -> p kc m", kc=16),
            in_=ih["wproj"].rearrange("(kc p) m -> p kc m", p=128)),
            stream=ld)
        ctx.dop("g", lambda g: g.dma_start(
            out=a["cwf"].ap().rearrange("p (ct k) -> p ct k", ct=8),
            in_=ih["convw"].rearrange("(ct p) k -> p ct k", p=128)),
            stream=ld)
        m_ld = (ld, ld.n)
        bw = self.bank_wait("pF1")
        pF1 = a["pF1"]
        mm = self.MM(pF1[:, 0:152], c["ones1"][:, :], c["row4"][0:1, 0:152], True, True,
                     self.ms["cload"] + bw, tag="p0bc")
        mnw = self.CP("v", c["nwrep"][:, :], pF1[:, 0:128], [mm])
        mex = self.ACT(c["nAerep"][:, 0:4], pF1[:, 132:136], AF.Exp, [mm])
        mex = self.TS("v", c["nAerep"][:, 0:4], c["nAerep"][:, 0:4], -1.0, None, ALU.mult, None,
                      [mex])
        mdt = self.CP("v", c["dtbrep"][:, 0:4], pF1[:, 128:132], [mm])
        mrb = self.CP("v", c["rbias"][:, :], pF1[:, 136:152], [mm])
        for j in range(1, NCS * 2):
            self.CP("v", c["nAerep"][:, 4 * j:4 * (j + 1)], c["nAerep"][:, 0:4], [mex])
            self.CP("v", c["dtbrep"][:, 4 * j:4 * (j + 1)], c["dtbrep"][:, 0:4], [mdt])
        m_bc = ctx.mark("v")
        self.bank_read("pF1", m_bc)
        self.bank_read("pF1", mex)
        self.bank_read("pF1", mrb)
        for j in range(32):
            self.TS("v", a["cdiag"][:, 128 * j:128 * (j + 1)], c["I_bf"][:, :],
                    a["cwf"][:, j:j + 1], None, ALU.mult, None, [m_ld] + self.ms["cload"])
        self.ctx.op("v", lambda v: v.memset(a["Sst"][:, :], 0.0), inc=True)
        self.ctx.op("v", lambda v: v.memset(a["Sbf"][:, :], 0.0), inc=True)
        self.ctx.op("v", lambda v: v.memset(a["halo"][:, :], 0.0), inc=True)
        self.ms["p0"] = ctx.mark("v")

    def emit_p1(self):
        ctx, a, ih, dr = self.ctx, self.a, self.ih, self.dr
        xl = ctx.stream("xld")
        ls = ctx.stream("ln1st")
        nrm_ms = [None, None]
        st_ms = [None, None]
        for i in range(TPC // 128):
            sl = i % 2
            xa = a["x1"][:, D * sl:D * (sl + 1)]
            lo = a["lno"][:, D * sl:D * (sl + 1)]
            w = [] if nrm_ms[sl] is None else [nrm_ms[sl]]
            mL = ctx.dop("sy", lambda sy, i=i, xa=xa: sy.dma_start(
                out=xa, in_=ih["x_my"][128 * i:128 * (i + 1), :]), waits=w, stream=xl)
            ms = self.emit_ln(xa, lo, mL, i % 4, a["dump"], a["st1"],
                              extra=[ls] if st_ms[sl] else [])
            nrm_ms[sl] = ms
            st_ms[sl] = ctx.dop("sy", lambda sy, i=i, lo=lo: sy.dma_start(
                out=dr["ag_in"][128 * i:128 * (i + 1), :], in_=lo), waits=[ms], stream=ls)
        self.ms["ln1st"] = (ls, ls.n)

    def emit_ln(self, x_ap, out_ap, m_ready, si, dump, st, extra=()):
        ctx = self.ctx
        if not isinstance(dump, bass.AP):
            dump = dump[:, :]
        s0 = 4 * si
        ssum = st[:, s0:s0 + 1]
        ssq = st[:, s0 + 1:s0 + 2]
        mean = st[:, s0 + 2:s0 + 3]
        var = st[:, s0 + 3:s0 + 4]
        m1 = ctx.op("v", lambda v: v.reduce_sum(ssum, x_ap, axis=AX), waits=[m_ready],
                    inc=True, tag="ln_sum")
        m2 = self.ACT(dump, x_ap, AF.Square, [m_ready], accum_out=ssq, tag="ln_sq")
        m3 = self.TS("v", mean, ssum, 1.0 / D, None, ALU.mult, None, [m1])
        m4 = self.TS("v", var, mean, mean, None, ALU.mult, None, [m3])
        m5 = self.TS("v", ssq, ssq, 1.0 / D, None, ALU.mult, None, [m2])
        m6 = self.TT("v", var, ssq, var, ALU.subtract, [m4, m5])
        m7 = self.TS("v", var, var, EPS_LN, None, ALU.add, None, [m6])
        m8 = ctx.op("v", lambda v: v.reciprocal(var, var), waits=[m7], inc=True)
        m9 = self.ACT(var, var, AF.Sqrt, [m8])
        mA = self.STT("v", mean, mean, -1.0, var, ALU.mult, ALU.mult, [m9])
        mB = self.TS("v", out_ap, x_ap, var, mean, ALU.mult, ALU.add, [mA] + list(extra))
        return mB

    def emit_p2(self):
        ctx, dr = self.ctx, self.dr
        cc1 = ctx.stream("cc1")
        ctx.op("g", lambda g: g.collective_compute(
            "AllGather", ALU.bypass, replica_groups=[list(range(NCORE))],
            ins=[dr["ag_in"][:, :].opt()], outs=[dr["ln1_all"][:, :].opt()]),
            waits=[self.ms["ln1st"]], inc=cc1, tag="ag_ln1")
        ctx.op("g", lambda g: g.collective_compute(
            "AllGather", ALU.bypass, replica_groups=[list(range(NCORE))],
            ins=[dr["bar_in"][:, :].opt()], outs=[dr["bar_out"][:, :].opt()]),
            waits=[], inc=cc1, tag="bar_ln1")
        self.ms["cc1"] = (cc1, 2)

    def emit_p3(self, mc):
        self.emit_p3_proj(mc)
        self.emit_p3_conv(mc)
        self.emit_p3_prep(mc)
        for cs in range(NCS):
            self.emit_p3_step(mc, cs)
        ctx, a = self.ctx, self.a
        if mc + 1 < NMC:
            w = [self.ms["conv_mm"], self.ctx.mark("t")]
            for bt in range(16):
                off = bt * (MEGA_S + 3)
                self.CP("v", a["halo"][:, bt * 3:bt * 3 + 3],
                        a["cin"][:, off + MEGA_S:off + MEGA_S + 3], w if bt == 0 else [])
            self.ms["halo_saved"] = ctx.mark("v")

    def emit_p3_proj(self, mc):
        ctx, a, dr = self.ctx, self.a, self.dr
        lnl = ctx.stream("lnl")
        s0 = MEGA_S * mc
        evac_w = [self.ms["p0"]] if mc == 0 else [self.ms["conv_mm"], self.ms["halo_saved"]]
        nb = 0
        lnbuf_last = [None, None]
        rnd = 0
        for b in range(2):
            for q in range(MEGA_S // 256):
                buf = rnd % 2
                rnd += 1
                tok0 = s0 + 256 * q
                wdma = [self.ms["cc1"]]
                if lnbuf_last[buf] is not None:
                    wdma.append(lnbuf_last[buf])
                for k in range(16):
                    ctx.dop("sy", lambda sy, b=b, k=k, tok0=tok0, buf=buf: sy.dma_start(
                        out=a["lnfm"][:, buf * 4096 + 256 * k:buf * 4096 + 256 * k + 256],
                        in_=dr["ln1_all"][b * S + tok0:b * S + tok0 + 256, 128 * k:128 * (k + 1)],
                        transpose=True), waits=wdma, stream=lnl, tag="lnfm")
                m_lnl = (lnl, lnl.n)
                for mt in range(13):
                    bank = "pF%d" % (nb % 6 + 1)
                    nb += 1
                    pb = a[bank]
                    mw = 128 if mt < 12 else 8
                    bwk = self.bank_wait(bank)
                    for k in range(16):
                        mmw = ([m_lnl] + bwk + evac_w) if k == 0 else []
                        self.MM(pb[0:mw, 0:256],
                                a["wproj"][:, PROJ_PC * k + 128 * mt:PROJ_PC * k + 128 * mt + mw],
                                a["lnfm"][:, buf * 4096 + 256 * k:buf * 4096 + 256 * k + 256],
                                k == 0, k == 15, mmw, tag="projmm")
                    m_mm = ctx.mark("t")
                    eng = "v" if (mt % 2 == 0) else "sc"
                    me = self.emit_proj_evac(b, mt, 256 * q, pb, m_mm, eng)
                    self.bank_read(bank, me)
                lnbuf_last[buf] = ctx.mark("t")
        self.ms["proj_evac"] = [ctx.mark("v"), ctx.mark("sc")]

    def emit_proj_evac(self, b, mt, tok_off, pb, m_mm, eng):
        a = self.a
        if mt == 12:
            dst = a["bafm"][0:8, 512 * b + tok_off:512 * b + tok_off + 256]
            return self.CP(eng, dst, pb[0:8, 0:256], [m_mm], tag="ba_ev")
        h, r = mt // 6, mt % 6
        if r < 4:
            ct = h if r == 0 else (2 + h if r == 1 else 4 + 2 * h + (r - 2))
            off = (b * 8 + ct) * (MEGA_S + 3) + 3 + tok_off
            dst = a["cin"][:, off:off + 256]
        else:
            zt = 2 * h + (r - 4)
            off = (b * 4 + zt) * MEGA_S + tok_off
            dst = a["zst"][:, off:off + 256]
        return self.CP(eng, dst, pb[:, 0:256], [m_mm], tag="proj_ev")

    def emit_p3_conv(self, mc):
        ctx, a, c = self.ctx, self.a, self.c
        wv = self.ms["proj_evac"] + ([self.ms["p0"]] if mc == 0 else [self.ms["halo_saved"]])
        for bt in range(16):
            off = bt * (MEGA_S + 3)
            self.CP("v", a["cin"][:, off:off + 3], a["halo"][:, bt * 3:bt * 3 + 3],
                    wv if bt == 0 else [])
        m_halo = ctx.mark("v")
        nb = 0
        for b in range(2):
            for ct in range(8):
                for n in range(1):
                    bank = "pF%d" % (nb % 6 + 1)
                    nb += 1
                    pb = a[bank]
                    bwk = self.bank_wait(bank)
                    cin0 = (b * 8 + ct) * (MEGA_S + 3)
                    for k in range(4):
                        mmw = ([m_halo] + self.ms["proj_evac"] + bwk) if k == 0 else []
                        self.MM(pb[:, 0:MEGA_S],
                                a["cdiag"][:, (ct * 4 + k) * 128:(ct * 4 + k) * 128 + 128],
                                a["cin"][:, cin0 + k:cin0 + k + MEGA_S],
                                k == 0, k == 3, mmw, tag="convmm")
                    m_mm = ctx.mark("t")
                    cout0 = (b * 8 + ct) * MEGA_S
                    me = self.ACT(a["cout"][:, cout0:cout0 + MEGA_S], pb[:, 0:MEGA_S],
                                  AF.Silu, [m_mm], tag="convsilu")
                    self.bank_read(bank, me)
        self.ms["conv_mm"] = ctx.mark("t")
        self.ms["conv_done"] = ctx.mark("sc")

    def emit_p3_prep(self, mc):
        ctx, a, c = self.ctx, self.a, self.c
        wv = self.ms["proj_evac"]
        nb = 0
        for cs in range(NCS):
            for b in range(2):
                bank = "pF%d" % (nb % 2 + 1)
                nb += 1
                pb = a[bank]
                bwk = self.bank_wait(bank)
                mt = self.TR(pb[0:128, 0:8],
                             a["bafm"][0:8, 512 * b + 128 * cs:512 * b + 128 * cs + 128],
                             c["I_f"][0:8, 0:8], wv + bwk, tag="batr")
                col = cs * 8 + b * 4
                m1 = self.CP("v", a["bb"][:, col:col + 4], pb[0:128, 0:4], [mt])
                m2 = self.CP("v", a["aa"][:, col:col + 4], pb[0:128, 4:8], [mt])
                self.bank_read(bank, m1)
                self.bank_read(bank, m2)
        m_ba = ctx.mark("v")
        mbeta = self.ACT(a["beta"][:, :], a["bb"][:, :], AF.Sigmoid, [m_ba])
        ngf0 = ctx.stream("ngf")
        m1 = self.TT("v", a["gt1"][:, :], a["aa"][:, :], c["dtbrep"][:, :], ALU.add,
                     [m_ba, self.ms["p0"], (ngf0, ngf0.n)])
        m2 = self.TS("v", a["gt3"][:, :], a["gt1"][:, :], 0.0, None, ALU.max, None, [m1])
        m3 = self.ACT(a["gt1"][:, :], a["gt1"][:, :], AF.Abs, [m2])
        m4 = self.ACT(a["gt1"][:, :], a["gt1"][:, :], AF.Exp, [m3], scale=-1.0)
        m5 = self.ACT(a["gt1"][:, :], a["gt1"][:, :], AF.Ln, [m4], bias=1.0)
        m6 = self.TT("v", a["gt1"][:, :], a["gt1"][:, :], a["gt3"][:, :], ALU.add, [m5])
        m7 = self.TT("v", a["gt1"][:, :], a["gt1"][:, :], c["nAerep"][:, :], ALU.mult, [m6])
        bwk = self.bank_wait("pF3")
        mg = self.MM(a["pF3"][:, 0:NCS * 8], c["triuIF"][:, :], a["gt1"][:, :], True, True,
                     [m7] + bwk, tag="cumsum")
        mgc = self.CP("v", a["gc"][:, :], a["pF3"][:, 0:NCS * 8], [mg])
        self.bank_read("pF3", mgc)
        mngc = self.TS("v", a["ngc"][:, :], a["gc"][:, :], -1.0, None, ALU.mult, None, [mgc])
        megc = self.ACT(a["egc"][:, :], a["gc"][:, :], AF.Exp, [mgc])
        bwk = self.bank_wait("pF4")
        mt = self.TR(a["pF4"][0:NCS * 8, 0:128], a["ngc"][:, :], c["I_f"][:, :], [mngc] + bwk,
                     tag="ngcT")
        mc2 = self.CP("v", a["gt2"][0:NCS * 8, 0:128], a["pF4"][0:NCS * 8, 0:128], [mt])
        self.bank_read("pF4", mc2)
        ngf = ctx.stream("ngf")
        mf0 = ctx.dop("g", lambda g: g.dma_start(
            out=self.dr["ngscr"][0:NCS * 8, :], in_=a["gt2"][0:NCS * 8, 0:128]),
            waits=[mc2, self.ctx.mark("t")], stream=ngf, tag="ngscr")
        mfl = ctx.dop("g", lambda g: g.dma_start(
            out=a["ngcflat"][0:65:64, :],
            in_=self.dr["ngscr"][0:NCS * 8, :].rearrange("(a b) m -> a (b m)", a=2)),
            waits=[mf0], stream=ngf, tag="ngcflat")
        self.ms["prep"] = [mbeta, megc, mfl, ctx.mark("v")]

    def emit_p3_step(self, mc, cs):
        ctx, a, c = self.ctx, self.a, self.c
        TT, TS, STT, CP, TR, MM, ACT = (self.TT, self.TS, self.STT, self.CP, self.TR,
                                        self.MM, self.ACT)
        col0 = cs * 8
        prep_w = self.ms["prep"]
        conv_w = [self.ms["conv_done"]]
        sl = 128 * cs
        if cs == 0:
            for e in ("t", "v", "sc"):
                ctx.pending[e] = ctx.pending[e] + list(prep_w) + conv_w

        bwk = self.bank_wait("pT")
        for b in range(2):
            for j in range(4):
                off = (b * 8 + j) * MEGA_S + sl
                self.TR(a["pT"][:, (b * 4 + j) * 128:(b * 4 + j) * 128 + 128],
                        a["cout"][:, off:off + 128], c["I_bf"][:, :],
                        (conv_w + bwk) if (b == 0 and j == 0) else [])
        m_trqk = ctx.mark("t")
        bwk = self.bank_wait("pT2")
        for b in range(2):
            for vv in range(4):
                off = (b * 8 + 4 + vv) * MEGA_S + sl
                self.TR(a["pT2"][:, (b * 4 + vv) * 128:(b * 4 + vv) * 128 + 128],
                        a["cout"][:, off:off + 128], c["I_bf"][:, :],
                        (conv_w + bwk) if (b == 0 and vv == 0) else [])
        m_trv = ctx.mark("t")
        mv_ = CP("sc", a["v_tm"][:, :], a["pT2"][:, :], [m_trv, self.ctx.mark("v")])
        self.bank_read("pT2", mv_)
        bwk = self.bank_wait("pT2")
        for b in range(2):
            for vv in range(4):
                off = (b * 4 + vv) * MEGA_S + sl
                self.TR(a["pT2"][:, (b * 4 + vv) * 128:(b * 4 + vv) * 128 + 128],
                        a["zst"][:, off:off + 128], c["I_bf"][:, :],
                        bwk if (b == 0 and vv == 0) else [])
        m_trz = ctx.mark("t")
        mz_ = CP("v", a["z_tm"][:, :], a["pT2"][:, :], [m_trz, self.ctx.mark("sc")])
        self.bank_read("pT2", mz_)

        m_sq = []
        for b in range(2):
            for j in range(4):
                src = a["pT"][:, (b * 4 + j) * 128:(b * 4 + j) * 128 + 128]
                sidx = (2 * b + j) if j < 2 else (4 + 2 * b + (j - 2))
                m = ACT(a["ttmp"][:, (b * 4 + j) * 128:(b * 4 + j) * 128 + 128], src,
                        AF.Square,
                        [m_trqk, ctx.mark("v")] if b == 0 and j == 0 else [],
                        accum_out=a["ssq8"][:, sidx:sidx + 1], tag="rms_sq")
                m_sq.append(m)
        m1 = TS("v", a["ssq8"][:, 8:16], a["ssq8"][:, 0:8], 1.0 / HD, EPS_RMS, ALU.mult,
                ALU.add, [m_sq[-1]])
        m2 = ctx.op("v", lambda v: v.reciprocal(a["ssq8"][:, 16:24], a["ssq8"][:, 8:16]),
                    waits=[m1], inc=True)
        mq = ACT(a["ssq8"][:, 24:28], a["ssq8"][:, 16:20], AF.Sqrt, [m2],
                 scale=1.0 / (HD * HD))
        mk = ACT(a["ssq8"][:, 28:32], a["ssq8"][:, 20:24], AF.Sqrt, [mq], scale=1.0 / HD)
        m_app_last = None
        for b in range(2):
            for j in range(4):
                src = a["pT"][:, (b * 4 + j) * 128:(b * 4 + j) * 128 + 128]
                dst = a["qkh"][:, (b * 4 + j) * 128:(b * 4 + j) * 128 + 128]
                sidx = (24 + 2 * b + j) if j < 2 else (28 + 2 * b + (j - 2))
                m_app_last = TS("v", dst, src, a["ssq8"][:, sidx:sidx + 1], None, ALU.mult,
                                None, [mq if j < 2 else mk])
        m_qkh = ctx.mark("v")
        self.bank_read("pT", m_qkh)
        self.bank_read("pT", mk)

        bwk = self.bank_wait("pT")
        for i in range(8):
            self.TR(a["pT"][:, i * 128:i * 128 + 128], a["qkh"][:, i * 128:i * 128 + 128],
                    c["I_bf"][:, :], ([m_qkh] + bwk) if i == 0 else [])
        m_trf = ctx.mark("t")
        mqkf = CP("sc", a["qkhf"][:, :], a["pT"][:, :], [m_trf])
        self.bank_read("pT", mqkf)

        bw1 = self.bank_wait("pF1")
        bw2 = self.bank_wait("pF2")
        for i in range(8):
            pb = a["pF1"] if i < 4 else a["pF2"]
            idx = col0 + i
            half = NCS * 4
            prow = 0 if idx < half else 64
            off = (idx % half) * 128
            self.MM(pb[:, (i % 4) * 128:(i % 4) * 128 + 128],
                    c["onesF"][prow:prow + 1, :],
                    a["ngcflat"][prow:prow + 1, off:off + 128], True, True,
                    (prep_w + bw1) if i == 0 else (bw2 if i == 4 else []), tag="RB")
        m_rb = ctx.mark("t")
        mega1 = ACT(a["eglast"][:, 0:4],
                    a["pF1"].ap().rearrange("p (r m) -> p r m", r=4)[:, 0:4, 127],
                    AF.Exp, [m_rb], scale=-1.0)
        mega2 = ACT(a["eglast"][:, 4:8],
                    a["pF2"].ap().rearrange("p (r m) -> p r m", r=4)[:, 0:4, 127],
                    AF.Exp, [mega1], scale=-1.0)
        m_D = []
        for i in range(8):
            pb = a["pF1"] if i < 4 else a["pF2"]
            m = TS("v", a["D"][:, i * 128:i * 128 + 128],
                   pb[:, (i % 4) * 128:(i % 4) * 128 + 128],
                   a["gc"][:, col0 + i:col0 + i + 1], None, ALU.add, None,
                   [m_rb, self.ctx.mark("sc")] if i == 0 else [])
            m_D.append(m)
        self.bank_read("pF1", m_D[3])
        self.bank_read("pF1", mega1)
        self.bank_read("pF2", m_D[7])
        self.bank_read("pF2", mega2)
        m_Dall = ctx.mark("v")
        medec = ACT(a["edec"][:, 0:8], r8(a["D"].ap())[:, 0:8, 127], AF.Exp, [m_Dall],
                    scale=-1.0)
        hst0 = ctx.stream("hst")
        mdm1 = TT("v", r8(a["Dm"].ap()), r8(a["D"].ap()), self.bc(c["strilF"][:, :], 8),
                  ALU.mult, [m_Dall, self.ctx.mark("sc"), (hst0, hst0.n)])
        mel = ACT(a["EL"][:, :], a["Dm"][:, :], AF.Exp, [mdm1], tag="EL")
        mdm2 = TT("v", r8(a["Dm"].ap()), r8(a["D"].ap()), self.bc(c["striuF"][:, :], 8),
                  ALU.mult, [mel])
        meu = ACT(a["EU"][:, :], a["Dm"][:, :], AF.Exp, [mdm2], scale=-1.0, tag="EU")

        bw3 = self.bank_wait("pF3")
        bw4 = self.bank_wait("pF4")
        for b in range(2):
            for hq in range(2):
                kcol = (b * 4 + 2 + hq) * 128
                qcol = (b * 4 + hq) * 128
                j = b * 2 + hq
                self.MM(a["pF3"][:, j * 128:j * 128 + 128], a["qkhf"][:, kcol:kcol + 128],
                        a["qkhf"][:, kcol:kcol + 128], True, True,
                        ([mqkf] + bw3) if j == 0 else [], tag="KKT")
                self.MM(a["pF4"][:, j * 128:j * 128 + 128], a["qkhf"][:, kcol:kcol + 128],
                        a["qkhf"][:, qcol:qcol + 128], True, True,
                        bw4 if j == 0 else [], tag="KQT")
        m_kk = ctx.mark("t")

        m_n = []
        m_at = []
        for i in range(8):
            b, vv = i // 4, i % 4
            hq = vv // 2
            j = b * 2 + hq
            bcol = a["beta"][:, col0 + i:col0 + i + 1]
            mt1 = TT("v", a["ttmp"][:, i * 128:i * 128 + 128],
                     a["EL"][:, i * 128:i * 128 + 128],
                     a["pF3"][:, j * 128:j * 128 + 128], ALU.mult, [mel, m_kk])
            mt2 = STT("v", a["Nm"][:, i * 128:i * 128 + 128],
                      a["ttmp"][:, i * 128:i * 128 + 128], bcol, c["strilB"][:, :],
                      ALU.mult, ALU.mult, [])
            m_n.append(mt2)
            ma1 = TT("v", a["kbe"][:, i * 128:i * 128 + 128],
                     a["EU"][:, i * 128:i * 128 + 128],
                     a["pF4"][:, j * 128:j * 128 + 128], ALU.mult, [meu, m_kk])
            ma2 = TT("v", a["attnT"][:, i * 128:i * 128 + 128],
                     a["kbe"][:, i * 128:i * 128 + 128], c["triuIB"][:, :], ALU.mult, [])
            m_at.append(ma2)
        self.bank_read("pF3", m_n[-1])
        self.bank_read("pF4", m_at[-1])
        bwk = self.bank_wait("pT2")
        for i in range(8):
            self.TR(a["pT2"][:, i * 128:i * 128 + 128], a["Nm"][:, i * 128:i * 128 + 128],
                    c["I_bf"][:, :], ([m_n[-1]] + bwk) if i == 0 else [])
        m_ntr = ctx.mark("t")
        mnt = CP("v", a["NTm"][:, :], a["pT2"][:, :], [m_ntr])
        self.bank_read("pT2", mnt)
        mq0 = TT("v", r8(a["Q"].ap()[:, 0:1024]), self.bc(c["I_bf"][:, :], 8),
                 r8(a["NTm"].ap()), ALU.subtract, [mnt], tag="Q0")

        Fprev, Pprev = a["Nm"].ap(), a["NTm"].ap()
        m_fp = [m_n[-1], mnt]
        for j in range(1, NLEV):
            bwF = self.bank_wait("pF1") + self.bank_wait("pF2")
            bwP = self.bank_wait("pF3") + self.bank_wait("pF4")
            last = (j == NLEV - 1)
            for i in range(8):
                pbP = a["pF3"] if i < 4 else a["pF4"]
                if not last:
                    pbF = a["pF1"] if i < 4 else a["pF2"]
                    self.MM(pbF[:, (i % 4) * 128:(i % 4) * 128 + 128],
                            Pprev[:, i * 128:i * 128 + 128], Fprev[:, i * 128:i * 128 + 128],
                            True, True, (m_fp + bwF) if i == 0 else [], tag="sqF")
                self.MM(pbP[:, (i % 4) * 128:(i % 4) * 128 + 128],
                        Fprev[:, i * 128:i * 128 + 128], Pprev[:, i * 128:i * 128 + 128],
                        True, True, (m_fp + bwP) if i == 0 else [], tag="sqP")
            m_sq2 = ctx.mark("t")
            Fn = a["Fp"].ap()[:, (j % 2) * 1024:(j % 2) * 1024 + 1024]
            Pn = a["Pp"].ap()[:, (j % 2) * 1024:(j % 2) * 1024 + 1024]
            mlist = []
            if not last:
                mf1 = CP("sc", Fn[:, 0:512], a["pF1"][:, :], [m_sq2])
                mf2 = CP("sc", Fn[:, 512:1024], a["pF2"][:, :], [mf1])
                self.bank_read("pF1", mf1)
                self.bank_read("pF2", mf2)
                mp1 = CP("v", Pn[:, 0:512], a["pF3"][:, :], [m_sq2])
                mp2 = CP("v", Pn[:, 512:1024], a["pF4"][:, :], [mp1])
                mlist += [mf1, mf2, mp1, mp2]
            mq1 = TT("v", a["Q"].ap()[:, j * 1024:j * 1024 + 512]
                     .rearrange("p (r m) -> p r m", r=4),
                     a["pF3"].ap().rearrange("p (r m) -> p r m", r=4),
                     self.bc(c["I_bf"][:, :], 4), ALU.add, [m_sq2], tag="Qa")
            mq2 = TT("v", a["Q"].ap()[:, j * 1024 + 512:j * 1024 + 1024]
                     .rearrange("p (r m) -> p r m", r=4),
                     a["pF4"].ap().rearrange("p (r m) -> p r m", r=4),
                     self.bc(c["I_bf"][:, :], 4), ALU.add, [mq1], tag="Qb")
            mlist += [mq1, mq2]
            self.bank_read("pF3", mq2)
            self.bank_read("pF4", mq2)
            m_fp = [ctx.mark("v"), ctx.mark("sc")]
            Fprev, Pprev = Fn, Pn
        m_tsolve = m_fp + [mq0]

        mbe = TT("v", a["be8"][:, 0:8], a["beta"][:, col0:col0 + 8],
                 a["egc"][:, col0:col0 + 8], ALU.mult, [])
        m_kbe = []
        m_qt = []
        m_kd = []
        for i in range(8):
            b, vv = i // 4, i % 4
            hq = vv // 2
            kcol = (b * 4 + 2 + hq) * 128
            qcol = (b * 4 + hq) * 128
            m1 = TS("v", a["kbe"][:, i * 128:i * 128 + 128], a["qkh"][:, kcol:kcol + 128],
                    a["be8"][:, i:i + 1], None, ALU.mult, None,
                    [mbe, m_at[i]] if i == 0 else [m_at[i]])
            m_kbe.append(m1)
            m2 = TS("v", a["qt"][:, i * 128:i * 128 + 128], a["qkh"][:, qcol:qcol + 128],
                    a["egc"][:, col0 + i:col0 + i + 1], None, ALU.mult, None, [])
            m_qt.append(m2)
            m3 = TS("v", a["kd"][:, i * 128:i * 128 + 128], a["qkh"][:, kcol:kcol + 128],
                    a["edec"][:, i:i + 1], None, ALU.mult, None, [medec] if i == 0 else [])
            m_kd.append(m3)
        bwk = self.bank_wait("pT")
        for i in range(8):
            self.TR(a["pT"][:, i * 128:i * 128 + 128], a["kbe"][:, i * 128:i * 128 + 128],
                    c["I_bf"][:, :], ([m_kbe[-1]] + bwk) if i == 0 else [])
        m_tr1 = ctx.mark("t")
        mkbef = CP("sc", a["kbef"][:, :], a["pT"][:, :], [m_tr1])
        self.bank_read("pT", mkbef)
        bwk = self.bank_wait("pT2")
        for i in range(8):
            self.TR(a["pT2"][:, i * 128:i * 128 + 128], a["qt"][:, i * 128:i * 128 + 128],
                    c["I_bf"][:, :], ([m_qt[-1]] + bwk) if i == 0 else [])
        m_tr2 = ctx.mark("t")
        mqtf = CP("sc", a["qtf"][:, :], a["pT2"][:, :], [m_tr2])
        self.bank_read("pT2", mqtf)

        bw1 = self.bank_wait("pF1")
        bw2 = self.bank_wait("pF2")
        sbf_w = self.ms.get("sbf")
        for i in range(8):
            pb = a["pF1"] if i < 4 else a["pF2"]
            self.MM(pb[:, (i % 4) * 128:(i % 4) * 128 + 128],
                    a["kbef"][:, i * 128:i * 128 + 128], a["Sbf"][:, i * 128:i * 128 + 128],
                    True, True,
                    ([mkbef, sbf_w] + bw1) if i == 0 else (bw2 if i == 4 else []), tag="kbeS")
        m_kbes = ctx.mark("t")
        m_y0 = []
        for i in range(8):
            pb = a["pF1"] if i < 4 else a["pF2"]
            m = STT("v", a["Y"][:, i * 128:i * 128 + 128], a["v_tm"][:, i * 128:i * 128 + 128],
                    a["beta"][:, col0 + i:col0 + i + 1],
                    pb[:, (i % 4) * 128:(i % 4) * 128 + 128], ALU.mult, ALU.subtract,
                    [m_kbes, mv_] if i == 0 else [])
            m_y0.append(m)
        self.bank_read("pF1", m_y0[3])
        self.bank_read("pF2", m_y0[7])
        m_y = [ctx.mark("v")]

        Ycur, Ynxt = a["Y"].ap(), a["kbe"].ap()
        for lev in range(NLEV - 1, -1, -1):
            bw5 = self.bank_wait("pF5")
            bw6 = self.bank_wait("pF6")
            for i in range(8):
                pb = a["pF5"] if i < 4 else a["pF6"]
                self.MM(pb[:, (i % 4) * 128:(i % 4) * 128 + 128],
                        a["Q"][:, lev * 1024 + i * 128:lev * 1024 + i * 128 + 128],
                        Ycur[:, i * 128:i * 128 + 128], True, True,
                        (m_y + m_tsolve + bw5) if i == 0 else (bw6 if i == 4 else []),
                        tag="apply")
            m_ap = ctx.mark("t")
            mc1 = CP("v", Ynxt[:, 0:512], a["pF5"][:, :], [m_ap])
            mc2 = CP("sc", Ynxt[:, 512:1024], a["pF6"][:, :], [m_ap])
            self.bank_read("pF5", mc1)
            self.bank_read("pF6", mc2)
            m_y = [mc1, mc2]
            Ycur, Ynxt = Ynxt, Ycur
        v_new = Ycur

        bw3 = self.bank_wait("pF3")
        bw4 = self.bank_wait("pF4")
        for i in range(8):
            pb = a["pF3"] if i < 4 else a["pF4"]
            sls = pb[:, (i % 4) * 128:(i % 4) * 128 + 128]
            self.MM(sls, a["qtf"][:, i * 128:i * 128 + 128],
                    a["Sbf"][:, i * 128:i * 128 + 128], True, False,
                    ([mqtf, sbf_w] + bw3) if i == 0 else (bw4 if i == 4 else []), tag="out1")
            self.MM(sls, a["attnT"][:, i * 128:i * 128 + 128],
                    v_new[:, i * 128:i * 128 + 128], False, True,
                    m_y if i == 0 else [], tag="out2")
        m_out = ctx.mark("t")

        bw1 = self.bank_wait("pF1")
        bw2 = self.bank_wait("pF2")
        for i in range(8):
            pb = a["pF1"] if i < 4 else a["pF2"]
            self.MM(pb[:, (i % 4) * 128:(i % 4) * 128 + 128],
                    a["kd"][:, i * 128:i * 128 + 128], v_new[:, i * 128:i * 128 + 128],
                    True, True,
                    (m_y + [m_kd[-1]] + bw1) if i == 0 else (bw2 if i == 4 else []), tag="SU")
        m_su = ctx.mark("t")
        m_st = []
        for i in range(8):
            pb = a["pF1"] if i < 4 else a["pF2"]
            m = STT("v", a["Sst"][:, i * 128:i * 128 + 128], a["Sst"][:, i * 128:i * 128 + 128],
                    a["eglast"][:, i:i + 1], pb[:, (i % 4) * 128:(i % 4) * 128 + 128],
                    ALU.mult, ALU.add, [m_su, mega2] if i == 0 else [])
            m_st.append(m)
        self.bank_read("pF1", m_st[3])
        self.bank_read("pF2", m_st[7])
        msbf = CP("sc", a["Sbf"][:, :], a["Sst"][:, :], [m_st[-1], m_out], tag="sbf")
        self.ms["sbf"] = msbf

        m_gsq = []
        for i in range(8):
            pb = a["pF3"] if i < 4 else a["pF4"]
            m = ACT(a["ttmp"][:, i * 128:i * 128 + 128],
                    pb[:, (i % 4) * 128:(i % 4) * 128 + 128], AF.Square,
                    [m_out, ctx.mark("v")] if i == 0 else [],
                    accum_out=a["rst8"][:, i:i + 1], tag="gate_sq")
            m_gsq.append(m)
        mg1 = TS("v", a["rst8"][:, 8:16], a["rst8"][:, 0:8], 1.0 / HD, EPS_RMS, ALU.mult,
                 ALU.add, [m_gsq[-1]])
        mg2 = ctx.op("v", lambda v: v.reciprocal(a["rst8"][:, 16:24], a["rst8"][:, 8:16]),
                     waits=[mg1], inc=True)
        mg3 = ACT(a["rst8"][:, 16:24], a["rst8"][:, 16:24], AF.Sqrt, [mg2])
        msz = ACT(a["ttmp"][:, :], a["z_tm"][:, :], AF.Silu, [mz_, ctx.mark("v")])
        mgz = TT("v", r8(a["Dm"].ap()), r8(a["ttmp"].ap()), self.bc(c["nwrep"][:, :], 8),
                 ALU.mult, [msz, self.ms["p0"], (hst0, hst0.n)])
        m_h = []
        hst0 = ctx.stream("hst")
        for i in range(8):
            pb = a["pF3"] if i < 4 else a["pF4"]
            m = STT("v", a["h_sb"][:, i * 128:i * 128 + 128],
                    pb[:, (i % 4) * 128:(i % 4) * 128 + 128], a["rst8"][:, 16 + i:17 + i],
                    a["Dm"][:, i * 128:i * 128 + 128], ALU.mult, ALU.mult,
                    [mg3, mgz, (hst0, hst0.n)] if i == 0 else [], tag="h")
            m_h.append(m)
        self.bank_read("pF3", m_h[3])
        self.bank_read("pF4", m_h[7])

        hst = ctx.stream("hst")
        s_tok = MEGA_S * mc + sl
        bwk = self.bank_wait("pT")
        for i in range(8):
            self.TR(a["pT"][:, i * 128:i * 128 + 128], a["h_sb"][:, i * 128:i * 128 + 128],
                    c["I_bf"][:, :], ([m_h[-1]] + bwk) if i == 0 else [], tag="htr")
        m_htr = ctx.mark("t")
        mhf = CP("sc", a["qtf"][:, :], a["pT"][:, :], [m_htr])
        self.bank_read("pT", mhf)
        rs_slot_ms = getattr(self, "rs_slot_ms", {})
        self.rs_slot_ms = rs_slot_ms
        wo_slot_ms = getattr(self, "wo_slot_ms", {})
        self.wo_slot_ms = wo_slot_ms
        pbanks = ["pF1", "pF2", "pF3", "pF4"]
        rstg = a["Dm"].ap()
        for dt in range(4):
            ws = dt % 2
            wwait = [self.ms["precast"]] + ([wo_slot_ms[ws]] if ws in wo_slot_ms else [])
            mwo = ctx.dop("g", lambda g, dt=dt, ws=ws: g.dma_start(
                out=a["wosl"][:, ws * 2048:(ws + 1) * 2048]
                .rearrange("p (fc m) -> p fc m", fc=4),
                in_=self.dr["wosl_bf"][dt].rearrange("fc p m -> p fc m")),
                waits=wwait, stream=ctx.stream("wol%d" % ws), tag="wold")
            for b in range(2):
                rix = dt * 2 + b
                bank = pbanks[rix % 4]
                pb = a[bank]
                bwk = self.bank_wait(bank)
                for fc in range(4):
                    self.MM(pb[:, 0:512],
                            a["qtf"][:, b * 512 + fc * 128:b * 512 + fc * 128 + 128],
                            a["wosl"][:, ws * 2048 + fc * 512:ws * 2048 + (fc + 1) * 512],
                            fc == 0, fc == 3,
                            ([mhf, mwo, self.ms["p0"]] + bwk) if fc == 0 else [], tag="opmm")
                m_mm = ctx.mark("t")
                self.wo_slot_ms[ws] = m_mm
                slot = rix % 2
                wprev = [hst] if slot in rs_slot_ms else []
                eng = "v" if rix % 2 == 0 else "sc"
                me = CP(eng, rstg[:, slot * 512:(slot + 1) * 512], pb[:, 0:512],
                        [m_mm, m_h[-1]] + wprev, tag="opev")
                self.bank_read(bank, me)
                mdma = ctx.dop("sy", lambda sy, b=b, dt=dt, slot=slot, s_tok=s_tok:
                               sy.dma_start(
                    out=self.dr["rs_in%d" % dt][b * S + s_tok:b * S + s_tok + 128, :],
                    in_=rstg[:, slot * 512:(slot + 1) * 512]),
                    waits=[me], stream=hst, tag="rsstore")
                rs_slot_ms[slot] = mdma

    def emit_p4(self):
        ctx, dr = self.ctx, self.dr
        hst = ctx.stream("hst")
        cc2 = ctx.stream("cc2")
        for j in range(4):
            ctx.op("g", lambda g, j=j: g.collective_compute(
                "ReduceScatter", ALU.add, replica_groups=[list(range(NCORE))],
                ins=[dr["rs_in%d" % j][:, :].opt()],
                outs=[dr["rs_out%d" % j][:, :].opt()]),
                waits=[(hst, hst.n)] if j == 0 else [], inc=cc2, tag="rs_attn")
        ctx.op("g", lambda g: g.collective_compute(
            "AllGather", ALU.bypass, replica_groups=[list(range(NCORE))],
            ins=[dr["bar_in"][:, :].opt()], outs=[dr["bar_out2"][:, :].opt()]),
            waits=[], inc=cc2, tag="bar_attn")
        self.ms["cc2"] = (cc2, 5)

    def alloc_ffn(self, es):
        f = {}
        sb = self.sb
        f["xs"] = sb(es, "f_xs", 128, 4 * D, F32)
        f["dump"] = sb(es, "f_dump", 128, D, F32)
        f["atn"] = sb(es, "f_atn", 128, 2 * D, F32)
        f["ln2tm"] = sb(es, "f_ln2tm", 128, D, BF16)
        f["ln2fm"] = sb(es, "f_ln2fm", 128, 16 * 512, BF16)
        f["st2"] = sb(es, "f_st2", 128, 16, F32)
        f["rlog"] = sb(es, "f_rlog", 128, NEXP, F32)
        f["rexp"] = sb(es, "f_rexp", 128, NEXP, F32)
        f["rtmp"] = sb(es, "f_rtmp", 128, NEXP, F32)
        f["rsml"] = sb(es, "f_rsml", 128, 16, F32)
        f["wtil"] = sb(es, "f_wtil", 128, NEXP, F32)
        f["wtr"] = sb(es, "f_wtr", 16, 512, BF16)
        f["wflat"] = sb(es, "f_wflat", 1, 8192, BF16)
        f["wgu"] = sb(es, "f_wgu", 128, 4 * 2048, BF16)
        f["gsil"] = sb(es, "f_gsil", 128, 8 * 512, BF16)
        f["atile"] = sb(es, "f_atile", 128, 4 * 8 * 512, BF16)
        f["wdr"] = sb(es, "f_wdr", 128, 8 * 512, BF16)
        f["pT"] = self.pst(es, "f_pT", 128, 1024, BF16)
        f["pW"] = self.pst(es, "f_pW", 128, 512, F32)
        for i in range(1, 7):
            f["pF%d" % i] = self.pst(es, "f_pF%d" % i, 128, 512, F32)
        self.f = f
        self.bank_r = {k: [] for k in ("pT", "pW", "pF1", "pF2", "pF3", "pF4", "pF5", "pF6")}
        self.pools = f

    def emit_p5_init(self):
        pass

    def emit_p5(self, fm):
        ctx, f, c, ih, dr = self.ctx, self.f, self.c, self.ih, self.dr
        TT, TS, STT, CP, TR, MM, ACT = (self.TT, self.TS, self.STT, self.CP, self.TR,
                                        self.MM, self.ACT)
        row0 = fm * FFN_TOK
        ost = ctx.stream("ost")
        prev = self.ms.get("p5_x1")

        m_res = []
        for sub in range(4):
            w = [self.ms["cc2"]] + ([prev] if prev else [])
            mx = ctx.dop("sy", lambda sy, sub=sub: sy.dma_start(
                out=f["xs"][:, sub * D:(sub + 1) * D],
                in_=ih["x_my"][row0 + 128 * sub:row0 + 128 * (sub + 1), :]),
                waits=w, stream=ctx.stream("x5ld%d" % sub), tag="xld")
            sl2 = sub % 2
            wa = [self.ms["cc2"]]
            if sub >= 2:
                wa.append(m_res[sub - 2])
            ma = None
            for j in range(4):
                ma = ctx.dop("sy", lambda sy, sub=sub, sl2=sl2, j=j: sy.dma_start(
                    out=f["atn"][:, sl2 * D + j * 512:sl2 * D + (j + 1) * 512],
                    in_=dr["rs_out%d" % j][row0 + 128 * sub:row0 + 128 * (sub + 1), :]),
                    waits=wa if j == 0 else [], stream=ctx.stream("a5ld%d" % sub), tag="ald")
            m = TT("v", f["xs"][:, sub * D:(sub + 1) * D], f["xs"][:, sub * D:(sub + 1) * D],
                   f["atn"][:, sl2 * D:(sl2 + 1) * D], ALU.add, [mx, ma], tag="resadd")
            m_res.append(m)
        m_x1 = ctx.mark("v")
        if self.upto == "p5x":
            ost2 = ctx.stream("ost")
            last = None
            for sub in range(4):
                last = ctx.dop("sy", lambda sy, sub=sub: sy.dma_start(
                    out=self.out_h[row0 + 128 * sub:row0 + 128 * (sub + 1), :],
                    in_=f["xs"][:, sub * D:(sub + 1) * D]),
                    waits=[m_x1], stream=ost2, tag="ostore")
            self.ms["p5_x1x"] = last
        if self.upto == "p5o":
            ost2 = ctx.stream("ost")
            last = None
            for sub in range(4):
                last = ctx.dop("sy", lambda sy, sub=sub: sy.dma_start(
                    out=self.out_h[row0 + 128 * sub:row0 + 128 * (sub + 1), :],
                    in_=f["xs"][:, sub * D:(sub + 1) * D]),
                    waits=[m_x1], stream=ost2, tag="ostore")
            self.ms["p5_x1"] = last
            return

        m_rt = []
        for sub in range(4):
            x1s = f["xs"][:, sub * D:(sub + 1) * D]
            mB = self.emit_ln(x1s, f["ln2tm"][:, :], m_x1 if sub == 0 else m_rt[-1],
                              sub % 4, f["dump"], f["st2"], extra=[ctx.mark("t")])
            for g2 in range(2):
                bwk = self.bank_wait("pT")
                for j in range(8):
                    k = g2 * 8 + j
                    self.TR(f["pT"][:, j * 128:j * 128 + 128],
                            f["ln2tm"][:, k * 128:k * 128 + 128], c["I_bf"][:, :],
                            ([mB] + bwk) if j == 0 else [], tag="ln2tr")
                m_t = ctx.mark("t")
                me = CP("v", f["ln2fm"].ap().rearrange("p (k n) -> p k n", k=16)
                        [:, g2 * 8:(g2 + 1) * 8, sub * 128:sub * 128 + 128],
                        f["pT"].ap().rearrange("p (k n) -> p k n", k=8), [m_t], tag="ln2ev")
                self.bank_read("pT", me)
            if self.upto != "p5n":
                m_rt.append(self.emit_router(fm, sub, ctx.mark("v")))
            else:
                m_rt.append(ctx.mark("v"))
        m_lnfall = ctx.mark("v")
        if self.upto != "p5n":
            wfs = ctx.stream("wfs")
            mw0 = ctx.dop("g", lambda g: g.dma_start(
                out=dr["wscr"][:, :], in_=f["wtr"][:, :]),
                waits=[m_rt[-1]], stream=wfs, tag="wscr")
            mwf = ctx.dop("g", lambda g: g.dma_start(
                out=f["wflat"][0:1, :], in_=dr["wscr"].rearrange("p m -> (p m)").unsqueeze(0)),
                waits=[mw0], stream=wfs, tag="wflat")
        else:
            mwf = m_lnfall

        if self.upto in ("p5l", "p5n", "p5x"):
            ost2 = ctx.stream("ost")
            last = self.ms.get("p5_x1x")
            if self.upto != "p5x":
                for sub in range(4):
                    last = ctx.dop("sy", lambda sy, sub=sub: sy.dma_start(
                        out=self.out_h[row0 + 128 * sub:row0 + 128 * (sub + 1), :],
                        in_=f["xs"][:, sub * D:(sub + 1) * D]),
                        waits=[m_lnfall, mwf], stream=ost2, tag="ostore")
            else:
                mnop = ctx.op("sy", lambda sy: sy.nop(), waits=[m_lnfall, mwf], tag="gate")
            self.ms["p5_x1"] = last
            return

        if self.upto == "p5s":
            grps = [[16]]
        else:
            grps = [[0, 1, 2, 3], [4, 5, 6, 7], [8, 9, 10, 11], [12, 13, 14, 15], [16]]
        m_ffn = None
        for gi, grp in enumerate(grps):
            m_ffn = self.emit_ffn_grp(fm, gi, grp, m_lnfall, mwf)
        self.ms["ffn_done"] = m_ffn

        mstores = []
        for sub in range(4):
            m = ctx.dop("sy", lambda sy, sub=sub: sy.dma_start(
                out=self.out_h[row0 + 128 * sub:row0 + 128 * (sub + 1), :],
                in_=f["xs"][:, sub * D:(sub + 1) * D]),
                waits=[m_ffn], stream=ost, tag="ostore")
            mstores.append(m)
        self.ms["p5_x1"] = mstores[-1]

    def emit_router(self, fm, sub, m_ln):
        ctx, f, c = self.ctx, self.f, self.c
        TT, TS, CP, ACT = self.TT, self.TS, self.CP, self.ACT
        bwk = self.bank_wait("pF5")
        for k in range(16):
            self.MM(f["pF5"][:, 0:16],
                    f["ln2fm"][:, k * 512 + sub * 128:k * 512 + sub * 128 + 128],
                    c["wrT"][:, k * 16:(k + 1) * 16], k == 0, k == 15,
                    ([m_ln] + self.ms["cload"] + bwk) if k == 0 else [], tag="router")
        m_mm = ctx.mark("t")
        m1 = TT("v", f["rlog"][:, :], f["pF5"][:, 0:16], c["rbias"][:, :], ALU.add,
                [m_mm, self.ms["p0"]])
        self.bank_read("pF5", m1)
        m2 = ctx.op("v", lambda v: v.reduce_max(f["rsml"][:, 0:1], f["rlog"][:, :], axis=AX),
                    waits=[m1], inc=True)
        m3 = TS("v", f["rsml"][:, 1:2], f["rsml"][:, 0:1], -1.0, None, ALU.mult, None, [m2])
        m4 = ACT(f["rexp"][:, :], f["rlog"][:, :], AF.Exp, [m3], bias=f["rsml"][:, 1:2])
        m5 = ctx.op("v", lambda v: v.reduce_sum(f["rsml"][:, 2:3], f["rexp"][:, :], axis=AX),
                    waits=[m4], inc=True)
        m6 = ctx.op("v", lambda v: v.max(f["rsml"][:, 4:12], f["rexp"][:, :]), waits=[m4],
                    inc=True)
        m7 = TT("v", f["rsml"][:, 3:4], f["rsml"][:, 4:5], f["rsml"][:, 5:6], ALU.add, [m6])
        m8 = self.STT("v", f["rsml"][:, 3:4], f["rsml"][:, 2:3], EPS_TOP, f["rsml"][:, 3:4],
                      ALU.mult, ALU.add, [m7, m5])
        m9 = ctx.op("v", lambda v: v.reciprocal(f["rsml"][:, 3:4], f["rsml"][:, 3:4]),
                    waits=[m8], inc=True)
        mA = TS("v", f["rtmp"][:, :], f["rexp"][:, :], f["rsml"][:, 4:5], None, ALU.is_equal,
                None, [m6])
        mB = TS("v", f["wtil"][:, :], f["rexp"][:, :], f["rsml"][:, 5:6], None, ALU.is_equal,
                None, [mA])
        mC = TT("v", f["rtmp"][:, :], f["rtmp"][:, :], f["wtil"][:, :], ALU.max, [mB])
        mD = TT("v", f["rtmp"][:, :], f["rtmp"][:, :], f["rexp"][:, :], ALU.mult, [mC])
        mE = TS("v", f["wtil"][:, :], f["rtmp"][:, :], f["rsml"][:, 3:4], None, ALU.mult,
                None, [mD, m9])
        bwk = self.bank_wait("pF6")
        wfs0 = self.ctx.stream("wfs")
        mT = self.TR(f["pF6"][0:16, 0:128], f["wtil"][:, :], c["I_f"][:, :], [mE] + bwk,
                     tag="wtr")
        mV = CP("v", f["wtr"][0:16, sub * 128:sub * 128 + 128], f["pF6"][0:16, 0:128],
                [mT, (wfs0, wfs0.n)])
        self.bank_read("pF6", mV)
        return mV

    def emit_ffn_grp(self, fm, gi, grp, m_ln, m_wf):
        ctx, f, dr = self.ctx, self.f, self.dr
        TT, TS, CP, ACT = self.TT, self.TS, self.CP, self.ACT
        m_a = {}
        for ei, e in enumerate(grp):
            shared = (e == NEXP)
            wsrc_g = dr["swg_bf"] if shared else dr["wg_bf"][e]
            wsrc_u = dr["swu_bf"] if shared else dr["wu_bf"][e]
            if not shared:
                bwk = self.bank_wait("pW")
                mwb = self.MM(f["pW"][:, 0:512], self.c["ones1b"][:, :],
                              f["wflat"][0:1, e * 512:(e + 1) * 512], True, True,
                              [m_wf] + bwk, tag="wbc")
            else:
                mwb = None
            for mat in range(2):
                wsrc = wsrc_g if mat == 0 else wsrc_u
                for mh in range(2):
                    for mm4 in range(4):
                        m_col = mh * 4 + mm4
                        w = [self.ms["precast"]]
                        ru = self.ms.get(("wgu_reuse", mm4))
                        if ru:
                            w.append(ru)
                        glS = ctx.stream("wgul%d" % mm4)
                        ctx.dop("g", lambda g, wsrc=wsrc, m_col=m_col, mm4=mm4: g.dma_start(
                            out=f["wgu"][:, mm4 * 2048:(mm4 + 1) * 2048],
                            in_=wsrc[m_col]), waits=w, stream=glS,
                            tag="wgul")
                        bank = "pF%d" % (mm4 + 1)
                        bwk = self.bank_wait(bank)
                        for k in range(16):
                            w = ([glS, m_ln] + bwk) if k == 0 else []
                            self.MM(self.f[bank][:, 0:512],
                                    f["wgu"][:, mm4 * 2048 + k * 128:mm4 * 2048 + k * 128 + 128],
                                    f["ln2fm"][:, k * 512:(k + 1) * 512],
                                    k == 0, k == 15, w, tag="gu_mm")
                        m_mm = ctx.mark("t")
                        self.ms[("wgu_reuse", mm4)] = m_mm
                        mi = mh * 4 + mm4
                        if mat == 0:
                            me = ACT(f["gsil"][:, mi * 512:(mi + 1) * 512],
                                     self.f[bank][:, 0:512], AF.Silu,
                                     [m_mm, self.ctx.mark("v")], tag="gu_silu")
                        else:
                            me1 = TT("v", f["gsil"][:, mi * 512:(mi + 1) * 512],
                                     f["gsil"][:, mi * 512:(mi + 1) * 512],
                                     self.f[bank][:, 0:512], ALU.mult, [m_mm], tag="gu_mul")
                            if shared:
                                me = CP("v", f["atile"][:, (ei * 8 + mi) * 512:
                                                        (ei * 8 + mi + 1) * 512],
                                        f["gsil"][:, mi * 512:(mi + 1) * 512], [me1])
                            else:
                                me = TT("v", f["atile"][:, (ei * 8 + mi) * 512:
                                                        (ei * 8 + mi + 1) * 512],
                                        f["gsil"][:, mi * 512:(mi + 1) * 512],
                                        f["pW"][:, 0:512], ALU.mult, [me1, mwb], tag="gu_w")
                                self.bank_read("pW", me)
                        self.bank_read(bank, me)
            m_a[ei] = ctx.mark("v")
        m_aall = ctx.mark("v")

        m_last = None
        for dt in range(4):
            nmm = len(grp) * 8
            idx = 0
            slot_ms = {}
            for ei, e in enumerate(grp):
                shared = (e == NEXP)
                wsrc = dr["swd_bf"] if shared else dr["wd_bf"][e]
                for i8 in range(8):
                    slot = idx % 8
                    w = [self.ms["precast"]]
                    ru = self.ms.get(("wdr_reuse", slot))
                    if ru:
                        w.append(ru)
                    mload = ctx.dop("g", lambda g, wsrc=wsrc, dt=dt, i8=i8, slot=slot:
                                    g.dma_start(out=f["wdr"][:, slot * 512:(slot + 1) * 512],
                                                in_=wsrc[dt, i8]),
                                    waits=w, stream=ctx.stream("wdl%d" % slot),
                                    tag="wdl")
                    for sub in range(4):
                        bank = "pF%d" % (sub + 1)
                        bwk = self.bank_wait(bank) if idx == 0 else []
                        self.MM(self.f[bank][:, 0:512],
                                f["atile"][:, (ei * 8 + i8) * 512 + sub * 128:
                                           (ei * 8 + i8) * 512 + sub * 128 + 128],
                                f["wdr"][:, slot * 512:(slot + 1) * 512],
                                idx == 0, idx == nmm - 1,
                                ([mload, m_aall] if sub == 0 else []) + bwk, tag="dn_mm",
                                force_inc=(sub == 3))
                    self.ms[("wdr_reuse", slot)] = ctx.mark("t")
                    idx += 1
            m_mm = ctx.mark("t")
            for sub in range(4):
                bank = "pF%d" % (sub + 1)
                m = TT("v", f["xs"][:, sub * D + dt * 512:sub * D + dt * 512 + 512],
                       f["xs"][:, sub * D + dt * 512:sub * D + dt * 512 + 512],
                       self.f[bank][:, 0:512], ALU.add, [m_mm], tag="dn_add")
                self.bank_read(bank, m)
                m_last = m
        return m_last


_PROG_CACHE = {}


def _get_program():
    if "nc" not in _PROG_CACHE:
        _PROG_CACHE["nc"] = build_program()[0]
    return _PROG_CACHE["nc"]


def kernel(**inputs):
    x = np.asarray(inputs["L0_attn_x"], np.float32).reshape(T, D)
    Wq = np.asarray(inputs["L0_attn_W_qkvz"], np.float32)
    Wb = np.asarray(inputs["L0_attn_W_ba"], np.float32)
    cw = np.asarray(inputs["L0_attn_conv_w"], np.float32)
    dtb = np.asarray(inputs["L0_attn_dt_bias"], np.float32)
    alog = np.asarray(inputs["L0_attn_A_log"], np.float32)
    nw = np.asarray(inputs["L0_attn_norm_w"], np.float32)
    Wo = np.asarray(inputs["L0_attn_W_out"], np.float32)
    Wr = np.asarray(inputs["L0_ffn_Wr"], np.float32)
    rb = np.asarray(inputs["L0_ffn_bias"], np.float32)
    Wg = np.asarray(inputs["L0_ffn_Wg"], np.float32)
    Wu = np.asarray(inputs["L0_ffn_Wu"], np.float32)
    Wd = np.asarray(inputs["L0_ffn_Wd"], np.float32)
    sWg = np.asarray(inputs["L0_ffn_sWg"], np.float32)
    sWu = np.asarray(inputs["L0_ffn_sWu"], np.float32)
    sWd = np.asarray(inputs["L0_ffn_sWd"], np.float32)

    woutT = Wo.T
    wg_t = np.ascontiguousarray(
        Wg.reshape(NEXP, 16, 128, 8, 128).transpose(0, 3, 2, 1, 4)
        .reshape(NEXP, 8, 128, 2048))
    wu_t = np.ascontiguousarray(
        Wu.reshape(NEXP, 16, 128, 8, 128).transpose(0, 3, 2, 1, 4).reshape(NEXP, 8, 128, 2048))
    wd_t = np.ascontiguousarray(
        Wd.reshape(NEXP, 8, 128, 4, 512).transpose(0, 3, 1, 2, 4))
    swg_t = np.ascontiguousarray(
        sWg.reshape(16, 128, 8, 128).transpose(2, 1, 0, 3).reshape(8, 128, 2048))
    swu_t = np.ascontiguousarray(
        sWu.reshape(16, 128, 8, 128).transpose(2, 1, 0, 3).reshape(8, 128, 2048))
    swd_t = np.ascontiguousarray(sWd.reshape(8, 128, 4, 512).transpose(2, 0, 1, 3))
    wrT = np.ascontiguousarray(Wr.T)

    in_maps = []
    for cidx in range(NCORE):
        r0 = QKVZ_PC * cidx
        ba_rows = [8 * cidx + j for j in (0, 1, 4, 5, 2, 3, 6, 7)]
        wproj = np.ascontiguousarray(
            np.concatenate([Wq[r0:r0 + QKVZ_PC].T, Wb[ba_rows].T], axis=1))
        convw = np.ascontiguousarray(np.concatenate([
            cw[256 * cidx:256 * (cidx + 1)],
            cw[2048 + 256 * cidx:2048 + 256 * (cidx + 1)],
            cw[4096 + 512 * cidx:4096 + 512 * (cidx + 1)]], axis=0))
        wout_c = np.ascontiguousarray(
            woutT[512 * cidx:512 * (cidx + 1)].reshape(4, 128, 4, 512)
            .transpose(2, 0, 1, 3))
        in_maps.append({
            "x_my": np.ascontiguousarray(x[TPC * cidx:TPC * (cidx + 1)]),
            "wproj": wproj,
            "convw": convw,
            "dtb": np.ascontiguousarray(dtb[4 * cidx:4 * (cidx + 1)][None]),
            "alog": np.ascontiguousarray(alog[4 * cidx:4 * (cidx + 1)][None]),
            "normw": np.ascontiguousarray(nw[None]),
            "wout": wout_c,
            "wrT": wrT,
            "rbias": np.ascontiguousarray(rb[None]),
            "wg": wg_t, "wu": wu_t, "wd": wd_t,
            "swg": swg_t, "swu": swu_t, "swd": swd_t,
        })

    nc = _get_program()
    res = run_bass_kernel_spmd(nc, in_maps, core_ids=list(range(NCORE)))
    out = np.concatenate([res.results[cidx]["out"] for cidx in range(NCORE)], axis=0)
    return out.reshape(B, S, D)
